# version 09
# baseline (speedup 1.0000x reference)
import contextlib
import numpy as np

import concourse.bass as bass
import concourse.mybir as mybir
from concourse.bass_utils import run_bass_kernel_spmd

F32 = mybir.dt.float32
BF16 = mybir.dt.bfloat16
AF = mybir.ActivationFunctionType
ALU = mybir.AluOpType
AX = mybir.AxisListType

C = 2048
G = 16
H = 8192
NI = 5632
GI = 44
HG = 64
B = 2
T = 16384
TCORE = 4096
TT = 512
NTILE = TCORE // TT
EPS = 1e-5

BUILD_STAGE = 4
DEBUG_OUTS = []


class _Buf:
    __slots__ = ("name", "last_write", "readers")

    def __init__(self, name):
        self.name = name
        self.last_write = None
        self.readers = set()


class _Op:
    __slots__ = ("eng", "fn", "tag", "gidx", "deps", "need_inc", "sem_name",
                 "sem_val", "amount", "dma", "lane", "collective")

    def __init__(self, eng, fn, tag, gidx, dma, amount):
        self.eng = eng
        self.fn = fn
        self.tag = tag
        self.gidx = gidx
        self.dma = dma
        self.amount = amount
        self.deps = ()
        self.need_inc = False
        self.sem_name = None
        self.sem_val = None
        self.lane = 0
        self.collective = False


class Prog:

    ENGS = ("pe", "act", "dve", "pool", "sp")

    def __init__(self):
        self.streams = {e: [] for e in self.ENGS}
        self.gidx = 0
        self.barrier_set = set()
        self.dma_count = {e: 0 for e in self.ENGS}
        self.cc_count = 0

    def buf(self, name):
        return _Buf(name)

    def bufs_n(self, name, n):
        return [_Buf(f"{name}[{i}]") for i in range(n)]

    def add(self, eng, fn, reads=(), writes=(), tag="", dma=False, amount=None,
            collective=False):
        if amount is None:
            amount = 16 if dma else 1
        op = _Op(eng, fn, tag, self.gidx, dma, amount)
        op.collective = collective
        if collective:
            op.lane = f"cc{self.cc_count}"
            self.cc_count += 1
        elif dma:
            op.lane = self.dma_count[eng] % 2
            self.dma_count[eng] += 1
        self.gidx += 1
        deps = set()
        for b in reads:
            if b.last_write is not None:
                deps.add(b.last_write)
        for b in writes:
            if b.last_write is not None:
                deps.add(b.last_write)
            deps.update(b.readers)
        deps.update(self.barrier_set)
        deps.discard(op)
        deps = {d for d in deps
                if not (d.dma and op.dma and d.eng == op.eng
                        and not d.collective and not op.collective)}
        for d in deps:
            assert d.gidx < op.gidx, (d.tag, op.tag)
        op.deps = tuple(sorted(deps, key=lambda d: d.gidx))
        for b in reads:
            b.readers.add(op)
        for b in writes:
            b.last_write = op
            b.readers = set()
        self.streams[eng].append(op)
        return op

    def barrier(self):
        newset = set()
        for e in self.ENGS:
            ops = self.streams[e]
            for kind in (False, True):
                sel = [o for o in ops if o.dma == kind]
                if sel:
                    newset.add(sel[-1])
        self.barrier_set = newset

    def finalize(self, nc, final_eng="sp"):
        for e in self.ENGS:
            for op in self.streams[e]:
                for d in op.deps:
                    d.need_inc = True
        counters = {}
        order = sorted((op for e in self.ENGS for op in self.streams[e]),
                       key=lambda o: o.gidx)
        n_waits = 0
        for op in order:
            if op.dma:
                op.need_inc = True
                op.sem_name = f"q_{op.eng}{op.lane}"
            else:
                op.sem_name = "s_" + op.eng
            if op.need_inc:
                counters[op.sem_name] = counters.get(op.sem_name, 0) + op.amount
                op.sem_val = counters[op.sem_name]
        self.final_counts = dict(counters)

        sem_names = sorted(counters.keys())
        with contextlib.ExitStack() as st:
            sems = {n: st.enter_context(nc.semaphore(n)) for n in sem_names}
            with nc.Block() as block:
                def mk(ename):
                    def run(engobj):
                        wm = {}
                        for op in self.streams[ename]:
                            for d in op.deps:
                                if d.sem_val is None:
                                    continue
                                if wm.get(d.sem_name, 0) >= d.sem_val:
                                    continue
                                engobj.wait_ge(sems[d.sem_name], d.sem_val)
                                wm[d.sem_name] = d.sem_val
                            inst = op.fn(engobj)
                            if op.need_inc:
                                assert inst is not None, op.tag
                                inst.then_inc(sems[op.sem_name], op.amount)
                        if ename == final_eng:
                            for n, v in sorted(self.final_counts.items()):
                                engobj.wait_ge(sems[n], v)
                    return run

                block.tensor(mk("pe"))
                block.scalar(mk("act"))
                block.vector(mk("dve"))
                block.gpsimd(mk("pool"))
                block.sync(mk("sp"))


def _warp(W):
    A, Bi = W.shape[0] // 128, W.shape[1] // 128
    return np.ascontiguousarray(
        W.reshape(A, 128, Bi, 128).transpose(0, 3, 2, 1).reshape(A * 128, Bi * 128))


def _fm(vec):
    n = vec.shape[0] // 128
    return np.ascontiguousarray(vec.reshape(n, 128).T)


WNAMES = ("wk", "wv", "wr", "wo", "wcr", "wck", "wcv", "wg", "wu", "wd")


class Ctx:
    pass


def build_nc():
    nc = bass.Bass("TRN2", target_bir_lowering=False)
    p = Prog()
    X = Ctx()
    X.nc, X.p = nc, p
    dr = {}
    X.dr = dr

    def din(name, shape, dt=F32):
        dr[name] = nc.dram_tensor(name, shape, dt, kind="ExternalInput").ap()

    def dout(name, shape, dt=F32):
        dr[name] = nc.dram_tensor(name, shape, dt, kind="ExternalOutput").ap()

    def dint(name, shape, dt):
        dr[name] = nc.dram_tensor(name, shape, dt, kind="Internal").ap()

    din("xt", [128, NTILE * G * TT])
    din("xprev", [128, G])
    for nm in ("ln1w", "ln1b", "ln2w", "ln2b", "tmk", "tmv", "tmr",
               "cmk", "cmr", "td", "tf", "downs"):
        din(nm, [128, G])
    din("gates", [128, GI])
    din("ups", [128, GI])
    din("ident", [128, 128])
    din("ones", [128, 1])
    din("bflag", [128, 1])
    din("dc8", [128, 8])
    din("mask8", [128, 8])
    din("wk", [C, C]); din("wv", [C, C]); din("wr", [C, C]); din("wo", [C, C])
    din("wcr", [C, C]); din("wck", [H, C]); din("wcv", [C, H])
    din("wg", [NI, C]); din("wu", [NI, C]); din("wd", [C, NI])

    dout("y", [128, NTILE * G * TT])

    for nm in WNAMES:
        dint(nm + "_bf", list(dr[nm].shape), BF16)
    dint("p_spill", [128, G * TCORE], F32)
    dint("v_spill", [128, G * TCORE], F32)
    dint("sr_spill", [128, G * TCORE], F32)
    dint("x1_spill", [128, G * TCORE], F32)
    dint("xm_spill", [128, G * TCORE], F32)
    dint("at_spill", [128, G * TCORE], F32)
    dint("ck_spill", [128, G * TCORE], BF16)
    dint("srk_spill", [128, G * TCORE], F32)
    dint("xa_spill", [128, G * TCORE], F32)
    dint("fh_spill", [128, GI * TCORE], BF16)
    dint("cc1_in", [128, 32], F32)
    dint("cc1_out", [8 * 128, 32], F32)
    dint("cc2_in", [128, G], F32)
    dint("cc2_out", [8 * 128, G], F32)

    for d in DEBUG_OUTS:
        shapes = {"dbg_tot": [128, 32], "dbg_ain": [128, 32],
                  "dbg_bnd": [128, G],
                  "dbg_gath2": [128, 8 * G], "dbg_xmprev": [128, G],
                  "dbg_xmlast": [128, G],
                  "dbg_cc2in": [128, G]}
        if d.startswith("dbg_sp_"):
            nm = d[7:]
            ng = GI if nm == "fh" else G
            dt = BF16 if nm in ("ck", "fh") else F32
            dout(d, [128, ng * TT], dt)
        else:
            dout(d, shapes[d])

    top = contextlib.ExitStack()
    with top:
        PS = [top.enter_context(nc.psum_tensor(f"ps{i}", [128, 512], F32))
              for i in range(8)]
        psb = p.bufs_n("psum", 8)
        X.PS, X.psb = PS, psb

        def sb(name, shape, dt=F32):
            return top.enter_context(nc.sbuf_tensor(name, shape, dt, side="left"))

        ident = sb("identS", [128, 128])
        onesS = sb("onesS", [128, 1])
        cvec = {nm: sb(nm + "S", [128, G]) for nm in
                ("ln1w", "ln1b", "ln2w", "ln2b", "tmk", "tmv", "tmr",
                 "cmk", "cmr", "td", "tf", "downs", "e1", "lam", "eu")}
        gvec = {nm: sb(nm + "S", [128, GI]) for nm in ("gates", "ups")}
        bflag = sb("bflagS", [128, 1])
        dc8 = sb("dc8S", [128, 8])
        mask8 = sb("mask8S", [128, 8])
        epsT = sb("epsTS", [128, 1])
        X.epsT = epsT
        xnp_col = sb("xnp_colS", [128, G])
        xn2p_col = sb("xn2p_colS", [128, G])
        car1 = sb("car1S", [128, 32])
        ain = sb("ainS", [128, 32])
        car2 = sb("car2S", [128, 32])
        xn2carry = sb("xn2carryS", [128, G])
        gath1 = sb("gath1S", [128, 8 * 32])
        gath2 = sb("gath2S", [128, 8 * G])
        rows = sb("rowsS", [1, 8 * 512 + 64])
        rowsb = p.buf("rows")
        cscr = sb("cscrS", [128, 64])
        bcscr = p.buf("cscr")

        X.ident, X.onesS, X.cvec, X.gvec = ident, onesS, cvec, gvec
        X.bflag, X.dc8, X.mask8 = bflag, dc8, mask8
        X.xnp_col, X.xn2p_col = xnp_col, xn2p_col
        X.car1, X.ain, X.car2, X.xn2carry = car1, ain, car2, xn2carry
        X.gath1, X.gath2, X.rows, X.rowsb = gath1, gath2, rows, rowsb
        X.cscr, X.bcscr = cscr, bcscr

        consts_buf = p.buf("consts")
        X.consts = consts_buf
        X.bcar1 = p.bufs_n("car1", 32)
        X.bain = p.buf("ain")
        X.bcar2 = p.bufs_n("car2", 32)
        X.bxnp = p.buf("xnp_col")
        X.bxn2p = p.buf("xn2p_col")
        X.bxn2carry = p.buf("xn2carry")
        X.bgath1 = p.buf("gath1")
        X.bgath2 = p.buf("gath2")

        X.dbuf = {nm: p.buf("dram_" + nm) for nm in
                  ("cc1_in", "cc1_out", "cc2_in", "cc2_out")}
        X.dspill = {nm: p.bufs_n("dram_" + nm, NTILE)
                    for nm in ("p_spill", "v_spill", "sr_spill", "x1_spill",
                               "xm_spill", "at_spill",
                               "ck_spill", "srk_spill", "xa_spill", "fh_spill")}
        X.dwbf = {nm: p.buf("dram_" + nm + "_bf") for nm in WNAMES}

        def spsl(nm, tile, g0, g1, ng=G):
            base = tile * ng * TT
            return dr[nm][:, base + g0 * TT: base + g1 * TT]
        X.spsl = spsl

        def xsl(tile):
            return dr["xt"][:, tile * G * TT:(tile + 1) * G * TT]
        X.xsl = xsl

        def ysl(tile):
            return dr["y"][:, tile * G * TT:(tile + 1) * G * TT]
        X.ysl = ysl

        slot = lambda i: rows[0:1, i * 512:(i + 1) * 512]
        ROW = {"mu": slot(0), "msq": slot(1), "var": slot(1), "t": slot(2),
               "rstd": slot(3), "A1": slot(3), "aux": slot(4), "rstd2": slot(4),
               "A": slot(5), "Br": slot(6), "B1": slot(7),
               "r1": rows[0:1, 8 * 512:8 * 512 + 64]}
        X.ROW = ROW

        _emit_preamble(X)
        _emit_sweep1(X, top)
        if BUILD_STAGE >= 2:
            _emit_sweep2a(X, top)
        if BUILD_STAGE >= 3:
            _emit_sweep2bi(X, top)
        if BUILD_STAGE >= 4:
            _emit_sweep2bii(X, top)

        for d in DEBUG_OUTS:
            if d.startswith("dbg_sp_"):
                nm = d[7:] + "_spill"
                ng = GI if d[7:] == "fh" else G
                p.add("act", lambda e, d=d, nm=nm, ng=ng:
                      e.dma_start(out=dr[d], in_=spsl(nm, 0, 0, ng, ng=ng)),
                      reads=list(X.dspill[nm]), tag="dbg" + nm, dma=True)

        p.finalize(nc)
    return nc


def _emit_preamble(X):
    nc, p, dr = X.nc, X.p, X.dr
    consts = X.consts
    cvec = X.cvec

    def ld(dst_ap, src_ap, writes, eng="sp"):
        p.add(eng, lambda e, d=dst_ap, s=src_ap: e.dma_start(out=d, in_=s),
              writes=writes, tag="ld", dma=True)

    ld(X.ident[:], dr["ident"], [consts])
    ld(X.onesS[:], dr["ones"], [consts])
    for nm in ("ln1w", "ln1b", "ln2w", "ln2b", "tmk", "tmv", "tmr",
               "cmk", "cmr", "td", "tf", "downs"):
        ld(cvec[nm][:], dr[nm], [consts])
    for nm in ("gates", "ups"):
        ld(X.gvec[nm][:], dr[nm], [consts])
    ld(X.bflag[:], dr["bflag"], [consts])
    ld(X.dc8[:], dr["dc8"], [consts])
    ld(X.mask8[:], dr["mask8"], [consts])

    p.add("act", lambda e: e.activation(out=cvec["e1"][:], in_=cvec["td"][:],
                                        func=AF.Exp),
          reads=[consts], writes=[consts], tag="e1")
    p.add("act", lambda e: e.activation(out=cvec["lam"][:], in_=cvec["e1"][:],
                                        func=AF.Exp, scale=-1.0),
          reads=[consts], writes=[consts], tag="lam")
    p.add("act", lambda e: e.activation(out=cvec["eu"][:], in_=cvec["tf"][:],
                                        func=AF.Exp),
          reads=[consts], writes=[consts], tag="eu")
    p.add("dve", lambda e: e.memset(X.epsT[:], EPS),
          writes=[consts], tag="eps")

    for nm in WNAMES:
        p.add("pool", lambda e, d=dr[nm + "_bf"], s=dr[nm]:
              e.dma_start(out=d, in_=s),
              writes=[X.dwbf[nm]], tag=f"cast_{nm}", dma=True)

    PS, psb, ROW, rowsb = X.PS, X.psb, X.ROW, X.rowsb

    def row_stats(double):
        w = 512
        p.add("act", lambda e: e.activation(out=ROW["mu"], in_=PS[4][0:1, :w],
                                            func=AF.Identity, scale=1.0 / C),
              reads=[psb[4]], writes=[rowsb], tag="mu")
        p.add("act", lambda e: e.activation(out=ROW["msq"], in_=PS[5][0:1, :w],
                                            func=AF.Identity, scale=1.0 / C),
              reads=[psb[5]], writes=[rowsb], tag="msq")
        p.add("dve", lambda e: e.tensor_tensor(out=ROW["aux"], in0=ROW["mu"],
                                               in1=ROW["mu"], op=ALU.mult),
              reads=[rowsb], writes=[rowsb], tag="mu2")
        p.add("dve", lambda e: e.tensor_tensor(out=ROW["var"], in0=ROW["msq"],
                                               in1=ROW["aux"], op=ALU.subtract),
              reads=[rowsb], writes=[rowsb], tag="var")
        p.add("act", lambda e: e.activation(out=ROW["t"], in_=ROW["var"],
                                            func=AF.Ln, bias=X.epsT[0:1, 0:1]),
              reads=[rowsb], writes=[rowsb], tag="lnv")
        p.add("act", lambda e: e.activation(out=ROW["rstd"], in_=ROW["t"],
                                            func=AF.Exp, scale=-0.5),
              reads=[rowsb], writes=[rowsb], tag="rstd")
        if not double:
            p.add("dve", lambda e: e.tensor_copy(out=ROW["A"], in_=ROW["rstd"]),
                  reads=[rowsb], writes=[rowsb], tag="A")
            p.add("dve", lambda e: e.scalar_tensor_tensor(
                out=ROW["Br"], in0=ROW["mu"], scalar=-1.0, in1=ROW["rstd"],
                op0=ALU.mult, op1=ALU.mult),
                  reads=[rowsb], writes=[rowsb], tag="Br")
        else:
            p.add("act", lambda e: e.activation(out=ROW["aux"], in_=ROW["t"],
                                                func=AF.Exp, scale=-1.0),
                  reads=[rowsb], writes=[rowsb], tag="invv1")
            p.add("dve", lambda e: e.tensor_tensor(out=ROW["aux"], in0=ROW["var"],
                                                   in1=ROW["aux"], op=ALU.mult),
                  reads=[rowsb], writes=[rowsb], tag="var2")
            p.add("act", lambda e: e.activation(out=ROW["aux"], in_=ROW["aux"],
                                                func=AF.Ln, bias=X.epsT[0:1, 0:1]),
                  reads=[rowsb], writes=[rowsb], tag="lnv2")
            p.add("act", lambda e: e.activation(out=ROW["rstd2"], in_=ROW["aux"],
                                                func=AF.Exp, scale=-0.5),
                  reads=[rowsb], writes=[rowsb], tag="rstd2")
            p.add("dve", lambda e: e.tensor_tensor(out=ROW["A"], in0=ROW["rstd"],
                                                   in1=ROW["rstd2"], op=ALU.mult),
                  reads=[rowsb], writes=[rowsb], tag="A12")
            p.add("dve", lambda e: e.scalar_tensor_tensor(
                out=ROW["Br"], in0=ROW["mu"], scalar=-1.0, in1=ROW["A"],
                op0=ALU.mult, op1=ALU.mult),
                  reads=[rowsb], writes=[rowsb], tag="B12")
            p.add("dve", lambda e: e.scalar_tensor_tensor(
                out=ROW["B1"], in0=ROW["mu"], scalar=-1.0, in1=ROW["rstd"],
                op0=ALU.mult, op1=ALU.mult),
                  reads=[rowsb], writes=[rowsb], tag="B1")
    X.row_stats = row_stats

    def bcast_row(row_ap, bank, tag):
        p.add("pe", lambda e, r=row_ap, b=bank:
              e.matmul(out=PS[b][:, :512],
                       lhsT=X.ident[0:1, 0:1].broadcast_to([1, 128]),
                       rhs=r, start=True, stop=True),
              reads=[rowsb, X.consts], writes=[psb[bank]], tag="bc_" + tag)
    X.bcast_row = bcast_row

    def stats_into_ps45(src_aps, src_bufs, sq_scr, sq_bufs):
        for g in range(G):
            par = g % 2
            p.add("act", lambda e, g=g, par=par:
                  e.activation(out=sq_scr[par][:, :512], in_=src_aps[g],
                               func=AF.Square),
                  reads=[src_bufs[g]], writes=[sq_bufs[par]], tag=f"sq{g}")
            p.add("pe", lambda e, g=g:
                  e.matmul(out=PS[4][0:1, :512], lhsT=X.onesS[:], rhs=src_aps[g],
                           start=(g == 0), stop=(g == G - 1)),
                  reads=[src_bufs[g], X.consts], writes=[psb[4]], tag=f"st1_{g}")
            p.add("pe", lambda e, g=g, par=par:
                  e.matmul(out=PS[5][0:1, :512], lhsT=X.onesS[:],
                           rhs=sq_scr[par][:, :512],
                           start=(g == 0), stop=(g == G - 1)),
                  reads=[sq_bufs[par], X.consts], writes=[psb[5]], tag=f"st2_{g}")
    X.stats_into_ps45 = stats_into_ps45

    def col_ln(src_col, src_buf, dst_col, dst_buf, wname, bname, double, tag):
        r1 = ROW["r1"]
        p.add("act", lambda e: e.activation(out=X.cscr[:, 0:G], in_=src_col,
                                            func=AF.Square),
              reads=[src_buf], writes=[X.bcscr], tag=tag + "_sq")
        p.add("pe", lambda e: e.matmul(out=PS[6][0:1, 0:G], lhsT=X.onesS[:],
                                       rhs=src_col, start=True, stop=True),
              reads=[src_buf, X.consts], writes=[psb[6]], tag=tag + "_s1")
        p.add("pe", lambda e: e.matmul(out=PS[6][0:1, G:2 * G], lhsT=X.onesS[:],
                                       rhs=X.cscr[:, 0:G], start=True, stop=True),
              reads=[X.bcscr, X.consts], writes=[psb[6]], tag=tag + "_s2")
        p.add("dve", lambda e: e.tensor_reduce(out=r1[:, 0:1],
                                               in_=PS[6][0:1, 0:G],
                                               axis=AX.X, op=ALU.add),
              reads=[psb[6]], writes=[rowsb], tag=tag + "_r1")
        p.add("dve", lambda e: e.tensor_reduce(out=r1[:, 1:2],
                                               in_=PS[6][0:1, G:2 * G],
                                               axis=AX.X, op=ALU.add),
              reads=[psb[6]], writes=[rowsb], tag=tag + "_r2")
        p.add("act", lambda e: e.activation(out=r1[:, 2:3], in_=r1[:, 0:1],
                                            func=AF.Identity, scale=1.0 / C),
              reads=[rowsb], writes=[rowsb], tag=tag + "_mu")
        p.add("act", lambda e: e.activation(out=r1[:, 3:4], in_=r1[:, 1:2],
                                            func=AF.Identity, scale=1.0 / C),
              reads=[rowsb], writes=[rowsb], tag=tag + "_msq")
        p.add("dve", lambda e: e.tensor_tensor(out=r1[:, 4:5], in0=r1[:, 2:3],
                                               in1=r1[:, 2:3], op=ALU.mult),
              reads=[rowsb], writes=[rowsb], tag=tag + "_mu2")
        p.add("dve", lambda e: e.tensor_tensor(out=r1[:, 4:5], in0=r1[:, 3:4],
                                               in1=r1[:, 4:5], op=ALU.subtract),
              reads=[rowsb], writes=[rowsb], tag=tag + "_var")
        p.add("act", lambda e: e.activation(out=r1[:, 5:6], in_=r1[:, 4:5],
                                            func=AF.Ln, bias=X.epsT[0:1, 0:1]),
              reads=[rowsb], writes=[rowsb], tag=tag + "_ln")
        p.add("act", lambda e: e.activation(out=r1[:, 6:7], in_=r1[:, 5:6],
                                            func=AF.Exp, scale=-0.5),
              reads=[rowsb], writes=[rowsb], tag=tag + "_rstd")
        if double:
            p.add("act", lambda e: e.activation(out=r1[:, 7:8], in_=r1[:, 5:6],
                                                func=AF.Exp, scale=-1.0),
                  reads=[rowsb], writes=[rowsb], tag=tag + "_iv")
            p.add("dve", lambda e: e.tensor_tensor(out=r1[:, 7:8], in0=r1[:, 4:5],
                                                   in1=r1[:, 7:8], op=ALU.mult),
                  reads=[rowsb], writes=[rowsb], tag=tag + "_v2")
            p.add("act", lambda e: e.activation(out=r1[:, 7:8], in_=r1[:, 7:8],
                                                func=AF.Ln, bias=X.epsT[0:1, 0:1]),
                  reads=[rowsb], writes=[rowsb], tag=tag + "_ln2")
            p.add("act", lambda e: e.activation(out=r1[:, 8:9], in_=r1[:, 7:8],
                                                func=AF.Exp, scale=-0.5),
                  reads=[rowsb], writes=[rowsb], tag=tag + "_rstd2")
            p.add("dve", lambda e: e.tensor_tensor(out=r1[:, 6:7], in0=r1[:, 6:7],
                                                   in1=r1[:, 8:9], op=ALU.mult),
                  reads=[rowsb], writes=[rowsb], tag=tag + "_A")
        p.add("dve", lambda e: e.scalar_tensor_tensor(
            out=r1[:, 9:10], in0=r1[:, 2:3], scalar=-1.0, in1=r1[:, 6:7],
            op0=ALU.mult, op1=ALU.mult),
              reads=[rowsb], writes=[rowsb], tag=tag + "_B")
        p.add("pe", lambda e: e.matmul(
            out=PS[7][:, 0:1],
            lhsT=X.ident[0:1, 0:1].broadcast_to([1, 128]),
            rhs=r1[0:1, 6:7], start=True, stop=True),
              reads=[rowsb, X.consts], writes=[psb[7]], tag=tag + "_bA")
        p.add("pe", lambda e: e.matmul(
            out=PS[7][:, 2:3],
            lhsT=X.ident[0:1, 0:1].broadcast_to([1, 128]),
            rhs=r1[0:1, 9:10], start=True, stop=True),
              reads=[rowsb, X.consts], writes=[psb[7]], tag=tag + "_bB")
        p.add("dve", lambda e: e.tensor_copy(out=X.cscr[:, 32:35],
                                             in_=PS[7][:, 0:3]),
              reads=[psb[7]], writes=[X.bcscr], tag=tag + "_ev")
        p.add("dve", lambda e: e.tensor_scalar(
            out=dst_col, in0=src_col, scalar1=X.cscr[:, 32:33],
            scalar2=X.cscr[:, 34:35], op0=ALU.mult, op1=ALU.add),
              reads=[src_buf, X.bcscr], writes=[dst_buf], tag=tag + "_ap")
        if wname is not None:
            p.add("dve", lambda e: e.tensor_tensor(out=dst_col, in0=dst_col,
                                                   in1=cvec[wname][:],
                                                   op=ALU.mult),
                  reads=[dst_buf, X.consts], writes=[dst_buf], tag=tag + "_w")
            p.add("dve", lambda e: e.tensor_tensor(out=dst_col, in0=dst_col,
                                                   in1=cvec[bname][:],
                                                   op=ALU.add),
                  reads=[dst_buf, X.consts], writes=[dst_buf], tag=tag + "_b")
        p.add("dve", lambda e: e.tensor_scalar(out=dst_col, in0=dst_col,
                                               scalar1=X.bflag[:, 0:1],
                                               scalar2=None, op0=ALU.mult),
              reads=[dst_buf, X.consts], writes=[dst_buf], tag=tag + "_fl")
    X.col_ln = col_ln


def _emit_sweep1(X, top):
    nc, p, dr = X.nc, X.p, X.dr
    PS, psb = X.PS, X.psb
    cvec, ROW, rowsb = X.cvec, X.ROW, X.rowsb
    consts = X.consts
    car1, bcar1 = X.car1, X.bcar1

    s1 = contextlib.ExitStack()

    def rb(name, shape, dt=F32):
        return s1.enter_context(nc.sbuf_tensor(name, shape, dt, side="right"))

    xprev_col = rb("xprevC", [128, G])
    bxprev = p.buf("xprev")
    xfm = rb("xfm", [128, G * 513])
    bxfm = p.bufs_n("xfm_g", G)
    sqs = [rb(f"sqs{i}", [128, 512]) for i in range(2)]
    bsqs = p.bufs_n("sqs", 2)
    x1h = rb("x1h", [128, 8 * TT])
    bx1h = p.bufs_n("x1h_g", 8)
    xkb = rb("xkb", [128, G * TT], BF16)
    bxk = p.bufs_n("xk_g", G)
    xvb = rb("xvb", [128, G * TT], BF16)
    bxv = p.bufs_n("xv_g", G)
    xrb = rb("xrb1", [128, G * TT], BF16)
    bxr = p.bufs_n("xr1_g", G)
    pbuf = rb("pbuf", [128, G * TT])
    bp = p.bufs_n("p_g", G)
    vbuf = rb("vbuf", [128, 4 * TT])
    bv = p.bufs_n("v_g", 4)
    srb = rb("srb", [128, 4 * TT])
    bsr = p.bufs_n("sr_g", 4)
    wbuf = [rb(f"wbuf{i}", [128, C], BF16) for i in range(2)]
    bwbuf = p.bufs_n("wbuf", 2)
    scr1 = rb("scr1", [128, 512])
    bscr1 = p.buf("scr1")
    pvs = [rb(f"pvs_{i}", [128, 512]) for i in range(2)]
    bpvs = p.bufs_n("pvs", 2)
    scana = [rb(f"scana{i}", [128, 512]) for i in range(2)]
    bscana = p.bufs_n("scana", 2)
    scanb = [rb(f"scanb{i}", [128, 512]) for i in range(2)]
    bscanb = p.bufs_n("scanb", 2)

    def xf_g(g, c0, c1):
        return xfm[:, g * 513 + c0: g * 513 + c1]

    p.add("sp", lambda e: e.dma_start(out=xprev_col[:], in_=dr["xprev"]),
          writes=[bxprev], tag="ld_xprev", dma=True)

    p.add("dve", lambda e: e.memset(car1[:], 0.0), writes=list(bcar1),
          tag="car1z")

    X.col_ln(xprev_col[:], bxprev, X.xnp_col[:], X.bxnp,
             "ln1w", "ln1b", True, "bxn")

    xfm_view = xfm[:].rearrange("p (g c) -> p g c", g=G)

    def prep_stats(tile):
        if tile == 0:
            p.add("dve", lambda e: e.tensor_copy(out=xfm_view[:, :, 0],
                                                 in_=X.xnp_col[:]),
                  reads=[X.bxnp], writes=list(bxfm), tag=f"bnd{tile}")
        else:
            p.add("dve", lambda e: e.tensor_copy(out=xfm_view[:, :, 0],
                                                 in_=xfm_view[:, :, 512]),
                  reads=list(bxfm), writes=list(bxfm), tag=f"bnd{tile}")

        p.add("sp", lambda e, tile=tile: e.dma_start(
            out=xfm_view[:, :, 1:513],
            in_=X.xsl(tile).rearrange("p (g t) -> p g t", g=G)),
              writes=list(bxfm), tag=f"ldx{tile}", dma=True)

        X.stats_into_ps45([xf_g(g, 1, 513) for g in range(G)], bxfm, sqs, bsqs)
        X.row_stats(double=True)
        X.bcast_row(ROW["A1"], 4, f"A1_{tile}")
        X.bcast_row(ROW["B1"], 5, f"B1_{tile}")

    def prep_x1xn(tile):
        X.bcast_row(ROW["rstd2"], 6, f"R2_{tile}")
        for h in range(2):
            for i in range(8):
                g = h * 8 + i
                p.add("dve", lambda e, g=g, i=i: e.tensor_tensor(
                    out=x1h[:, i * TT:(i + 1) * TT], in0=xf_g(g, 1, 513),
                    in1=PS[4][:, :512], op=ALU.mult),
                      reads=[bxfm[g], psb[4]], writes=[bx1h[i]], tag=f"x1a{g}")
                p.add("dve", lambda e, g=g, i=i: e.tensor_tensor(
                    out=x1h[:, i * TT:(i + 1) * TT],
                    in0=x1h[:, i * TT:(i + 1) * TT],
                    in1=PS[5][:, :512], op=ALU.add),
                      reads=[bx1h[i], psb[5]], writes=[bx1h[i]], tag=f"x1b{g}")
            p.add("act", lambda e, h=h, tile=tile: e.dma_start(
                out=X.spsl("x1_spill", tile, h * 8, h * 8 + 8), in_=x1h[:]),
                  reads=[bx1h[i] for i in range(8)],
                  writes=[X.dspill["x1_spill"][tile]],
                  tag=f"spx1_{tile}_{h}", dma=True)
            for i in range(8):
                g = h * 8 + i
                p.add("dve", lambda e, g=g, i=i: e.tensor_tensor(
                    out=xf_g(g, 1, 513), in0=x1h[:, i * TT:(i + 1) * TT],
                    in1=PS[6][:, :512], op=ALU.mult),
                      reads=[bx1h[i], psb[6]], writes=[bxfm[g]], tag=f"xn1{g}")
                p.add("dve", lambda e, g=g: e.tensor_scalar(
                    out=xf_g(g, 1, 513), in0=xf_g(g, 1, 513),
                    scalar1=cvec["ln1w"][:, g:g + 1],
                    scalar2=cvec["ln1b"][:, g:g + 1],
                    op0=ALU.mult, op1=ALU.add),
                      reads=[bxfm[g], consts], writes=[bxfm[g]], tag=f"xn2{g}")

    def mixes(tile):
        for g in range(G):
            p.add("dve", lambda e, g=g: e.tensor_tensor(
                out=scr1[:, :], in0=xf_g(g, 1, 513), in1=xf_g(g, 0, 512),
                op=ALU.subtract),
                  reads=[bxfm[g]], writes=[bscr1], tag=f"df_{g}")
            for nm, dst, dbufs in (("tmk", xkb, bxk), ("tmv", xvb, bxv),
                                   ("tmr", xrb, bxr)):
                p.add("dve", lambda e, g=g, nm=nm, dst=dst: e.scalar_tensor_tensor(
                    out=dst[:, g * TT:(g + 1) * TT], in0=scr1[:, :],
                    scalar=cvec[nm][:, g:g + 1], in1=xf_g(g, 0, 512),
                    op0=ALU.mult, op1=ALU.add),
                      reads=[bscr1, bxfm[g], consts], writes=[dbufs[g]],
                      tag=f"m{nm}_{g}")

    def wload(wnm, dg, wpar):
        blk = dr[wnm + "_bf"][dg * 128:(dg + 1) * 128, :]
        p.add("sp", lambda e, blk=blk, wpar=wpar:
              e.dma_start(out=wbuf[wpar][:], in_=blk),
              reads=[X.dwbf[wnm]], writes=[bwbuf[wpar]],
              tag=f"lw{wnm}{dg}", dma=True)

    def wmm(wnm, dg, wpar, bank, rhs, rhs_bufs):
        def mmg(e, wpar=wpar, bank=bank, rhs=rhs):
            last = None
            for cg in range(G):
                last = e.matmul(out=PS[bank][:, :512],
                                lhsT=wbuf[wpar][:, cg * 128:(cg + 1) * 128],
                                rhs=rhs[:, cg * TT:(cg + 1) * TT],
                                start=(cg == 0), stop=(cg == G - 1))
            return last
        p.add("pe", mmg, reads=[bwbuf[wpar]] + list(rhs_bufs),
              writes=[psb[bank]], tag=f"mm{wnm}{dg}")

    def stream_wk(tile):
        for dg in range(G):
            wpar = dg % 2
            bank = dg % 2
            wload("wk", dg, wpar)
            wmm("wk", dg, wpar, bank, xkb, bxk)
            p.add("act", lambda e, dg=dg, bank=bank: e.activation(
                out=pbuf[:, dg * TT:(dg + 1) * TT], in_=PS[bank][:, :512],
                func=AF.Exp),
                  reads=[psb[bank]], writes=[bp[dg]], tag=f"evk{dg}")
            sp_ = dg % 2
            p.add("dve", lambda e, g=dg, sp_=sp_: e.tensor_tensor_scan(
                out=scanb[sp_][:, :],
                data0=cvec["lam"][:, g:g + 1].broadcast_to([128, TT]),
                data1=pbuf[:, g * TT:(g + 1) * TT],
                initial=car1[:, 16 + g:17 + g],
                op0=ALU.mult, op1=ALU.add),
                  reads=[bp[dg], bcar1[16 + dg], consts],
                  writes=[bscanb[sp_]], tag=f"scb{dg}")
            p.add("dve", lambda e, g=dg, sp_=sp_: e.tensor_copy(
                out=car1[:, 16 + g:17 + g], in_=scanb[sp_][:, 511:512]),
                  reads=[bscanb[sp_]], writes=[bcar1[16 + dg]], tag=f"cwb{dg}")
            if dg == 7 or dg == 15:
                h0 = dg - 7
                p.add("act", lambda e, tile=tile, h0=h0: e.dma_start(
                    out=X.spsl("p_spill", tile, h0, h0 + 8),
                    in_=pbuf[:, h0 * TT:(h0 + 8) * TT]),
                      reads=[bp[h0 + i] for i in range(8)],
                      writes=[X.dspill["p_spill"][tile]],
                      tag=f"spp{tile}_{h0}", dma=True)

    def stream_wv(tile):
        for dg in range(G):
            wpar = dg % 2
            bank = 2 + dg % 2
            slot = dg % 4
            wload("wv", dg, wpar)
            wmm("wv", dg, wpar, bank, xvb, bxv)
            p.add("act", lambda e, slot=slot, bank=bank: e.activation(
                out=vbuf[:, slot * TT:(slot + 1) * TT], in_=PS[bank][:, :512],
                func=AF.Identity),
                  reads=[psb[bank]], writes=[bv[slot]], tag=f"evv{dg}")
            sp_ = dg % 2
            p.add("pool", lambda e, dg=dg, slot=slot, sp_=sp_: e.tensor_tensor(
                out=pvs[sp_][:, :], in0=pbuf[:, dg * TT:(dg + 1) * TT],
                in1=vbuf[:, slot * TT:(slot + 1) * TT], op=ALU.mult),
                  reads=[bp[dg], bv[slot]], writes=[bpvs[sp_]], tag=f"pv{dg}")
            p.add("dve", lambda e, g=dg, sp_=sp_: e.tensor_tensor_scan(
                out=scana[sp_][:, :],
                data0=cvec["lam"][:, g:g + 1].broadcast_to([128, TT]),
                data1=pvs[sp_][:, :], initial=car1[:, g:g + 1],
                op0=ALU.mult, op1=ALU.add),
                  reads=[bpvs[sp_], bcar1[dg], consts], writes=[bscana[sp_]],
                  tag=f"sca{dg}")
            p.add("dve", lambda e, g=dg, sp_=sp_: e.tensor_copy(
                out=car1[:, g:g + 1], in_=scana[sp_][:, 511:512]),
                  reads=[bscana[sp_]], writes=[bcar1[dg]], tag=f"cwa{dg}")
            if slot == 3:
                g0 = dg - 3
                p.add("act", lambda e, tile=tile, g0=g0: e.dma_start(
                    out=X.spsl("v_spill", tile, g0, g0 + 4), in_=vbuf[:]),
                      reads=[bv[i] for i in range(4)],
                      writes=[X.dspill["v_spill"][tile]],
                      tag=f"spv{tile}_{g0}", dma=True)

    def stream_wr(tile):
        for dg in range(G):
            wpar = dg % 2
            bank = dg % 2
            slot = dg % 4
            wload("wr", dg, wpar)
            wmm("wr", dg, wpar, bank, xrb, bxr)
            p.add("act", lambda e, slot=slot, bank=bank: e.activation(
                out=srb[:, slot * TT:(slot + 1) * TT], in_=PS[bank][:, :512],
                func=AF.Sigmoid),
                  reads=[psb[bank]], writes=[bsr[slot]], tag=f"evr{dg}")
            if slot == 3:
                g0 = dg - 3
                p.add("act", lambda e, tile=tile, g0=g0: e.dma_start(
                    out=X.spsl("sr_spill", tile, g0, g0 + 4), in_=srb[:]),
                      reads=[bsr[i] for i in range(4)],
                      writes=[X.dspill["sr_spill"][tile]],
                      tag=f"spsr{tile}_{g0}", dma=True)

    prep_stats(0)
    prep_x1xn(0)
    mixes(0)
    for tile in range(NTILE):
        stream_wk(tile)
        if tile + 1 < NTILE:
            prep_stats(tile + 1)
        stream_wv(tile)
        if tile + 1 < NTILE:
            prep_x1xn(tile + 1)
        stream_wr(tile)
        if tile + 1 < NTILE:
            mixes(tile + 1)

    if "dbg_tot" in DEBUG_OUTS:
        p.add("act", lambda e: e.dma_start(out=dr["dbg_tot"], in_=car1[:]),
              reads=list(bcar1), tag="dbg_tot", dma=True)

    p.add("pool", lambda e: e.dma_start(out=dr["cc1_in"], in_=car1[:]),
          reads=list(bcar1), writes=[X.dbuf["cc1_in"]], tag="cc1w", dma=True)
    p.add("pool", lambda e: e.collective_compute(
        "AllGather", ALU.bypass, replica_groups=[[0, 1, 2, 3, 4, 5, 6, 7]],
        ins=[dr["cc1_in"].opt()], outs=[dr["cc1_out"].opt()]),
          reads=[X.dbuf["cc1_in"]], writes=[X.dbuf["cc1_out"]], tag="ag1",
          dma=True, amount=1, collective=True)
    p.add("sp", lambda e: e.dma_start(
        out=X.gath1[:].rearrange("p (r c) -> p r c", r=8),
        in_=dr["cc1_out"].rearrange("(r p) c -> p r c", p=128)),
          reads=[X.dbuf["cc1_out"]], writes=[X.bgath1], tag="cc1r", dma=True)

    p.add("dve", lambda e: e.memset(X.ain[:], 0.0), writes=[X.bain], tag="ainz")
    for r in range(8):
        p.add("act", lambda e, r=r: e.activation(
            out=X.cscr[:, 0:G], in_=cvec["e1"][:], func=AF.Exp,
            scale=X.dc8[:, r:r + 1]),
              reads=[consts], writes=[X.bcscr], tag=f"coef{r}")
        for half in range(2):
            p.add("dve", lambda e, r=r, half=half: e.tensor_tensor(
                out=X.cscr[:, 32 + half * G:32 + (half + 1) * G],
                in0=X.cscr[:, 0:G],
                in1=X.gath1[:, r * 32 + half * G: r * 32 + (half + 1) * G],
                op=ALU.mult),
                  reads=[X.bcscr, X.bgath1], writes=[X.bcscr], tag=f"ct{r}_{half}")
            p.add("dve", lambda e, r=r, half=half: e.tensor_tensor(
                out=X.ain[:, half * G:(half + 1) * G],
                in0=X.ain[:, half * G:(half + 1) * G],
                in1=X.cscr[:, 32 + half * G:32 + (half + 1) * G], op=ALU.add),
                  reads=[X.bcscr, X.bain], writes=[X.bain], tag=f"ca{r}_{half}")
    if "dbg_ain" in DEBUG_OUTS:
        p.add("act", lambda e: e.dma_start(out=dr["dbg_ain"], in_=X.ain[:]),
              reads=[X.bain], tag="dbg_ain", dma=True)

    s1.close()
    p.barrier()


def _emit_sweep2a(X, top):
    nc, p, dr = X.nc, X.p, X.dr
    PS, psb = X.PS, X.psb
    cvec = X.cvec
    consts = X.consts

    s2 = contextlib.ExitStack()

    def rb(name, shape, dt=F32):
        return s2.enter_context(nc.sbuf_tensor(name, shape, dt, side="right"))

    pyb = rb("pyb", [128, 8 * TT])
    bpy = p.bufs_n("py_g", 8)
    vh = rb("vh2", [128, 8 * TT])
    bvh = p.bufs_n("vh2_g", 8)
    srh = rb("srh2", [128, 8 * TT])
    bsrh = p.bufs_n("srh2_g", 8)
    x1b = rb("x1b2", [128, G * TT])
    bx1 = p.bufs_n("x12_g", G)
    zab = [rb(f"zab{i}", [128, G * TT], BF16) for i in range(2)]
    bza = [p.bufs_n(f"za{i}_g", G) for i in range(2)]
    wevac = rb("wevac", [128, G * TT])
    bwe = p.bufs_n("wevac", G)
    wbuf = [rb(f"wbuf2{i}", [128, C], BF16) for i in range(2)]
    bwbuf = p.bufs_n("wbuf2", 2)
    a513 = [rb(f"a513_{i}", [128, 514]) for i in range(2)]
    ba513 = p.bufs_n("a513", 2)
    b513 = [rb(f"b513_{i}", [128, 514]) for i in range(2)]
    bb513 = p.bufs_n("b513", 2)
    pvs = [rb(f"pvs2_{i}", [128, 512]) for i in range(2)]
    bpvs = p.bufs_n("pvs2", 2)
    qscr = [rb(f"qscr{i}", [128, 512]) for i in range(2)]
    bq = p.bufs_n("qscr", 2)
    dscr = [rb(f"dscr{i}", [128, 512]) for i in range(2)]
    bd = p.bufs_n("dscr", 2)
    xmlast = rb("xmlast", [128, G])
    bxmlast = p.buf("xmlast")
    xmprev = rb("xmprev", [128, G])
    bxmprev = p.buf("xmprev")

    p.add("dve", lambda e: e.tensor_copy(out=X.car2[:], in_=X.ain[:]),
          reads=[X.bain], writes=list(X.bcar2), tag="car2i")

    def load_p(tile, h):
        p.add("sp", lambda e, h=h, tile=tile: e.dma_start(
            out=pyb[:], in_=X.spsl("p_spill", tile, h * 8, h * 8 + 8)),
              reads=[X.dspill["p_spill"][tile]],
              writes=list(bpy), tag=f"2ldp{tile}_{h}", dma=True)

    def load_vsr(tile, h):
        p.add("sp", lambda e, h=h, tile=tile: e.dma_start(
            out=vh[:], in_=X.spsl("v_spill", tile, h * 8, h * 8 + 8)),
              reads=[X.dspill["v_spill"][tile]],
              writes=list(bvh), tag=f"2ldv{tile}_{h}", dma=True)
        p.add("sp", lambda e, h=h, tile=tile: e.dma_start(
            out=srh[:], in_=X.spsl("sr_spill", tile, h * 8, h * 8 + 8)),
              reads=[X.dspill["sr_spill"][tile]],
              writes=list(bsrh), tag=f"2ldsr{tile}_{h}", dma=True)

    def load_x1(tile):
        for h in range(2):
            p.add("sp", lambda e, h=h, tile=tile: e.dma_start(
                out=x1b[:, h * 8 * TT:(h + 1) * 8 * TT],
                in_=X.spsl("x1_spill", tile, h * 8, h * 8 + 8)),
                  reads=[X.dspill["x1_spill"][tile]],
                  writes=[bx1[h * 8 + i] for i in range(8)],
                  tag=f"2ldx1_{tile}_{h}", dma=True)

    def wo_evac(src_tile, dg):
        bank = dg % 4
        p.add("act", lambda e, dg=dg, bank=bank: e.activation(
            out=wevac[:, dg * TT:(dg + 1) * TT], in_=PS[bank][:, :512],
            func=AF.Identity),
              reads=[psb[bank]], writes=[bwe[dg]], tag=f"2we{src_tile}_{dg}")

    def wo_add(src_tile, dg):
        p.add("pool", lambda e, dg=dg: e.tensor_tensor(
            out=x1b[:, dg * TT:(dg + 1) * TT],
            in0=x1b[:, dg * TT:(dg + 1) * TT],
            in1=wevac[:, dg * TT:(dg + 1) * TT], op=ALU.add),
              reads=[bx1[dg], bwe[dg]], writes=[bx1[dg]],
              tag=f"2xm{src_tile}_{dg}")
        if dg == 7 or dg == 15:
            h0 = dg - 7
            p.add("act", lambda e, tile=src_tile, h0=h0:
                  e.dma_start(out=X.spsl("xm_spill", tile, h0, h0 + 8),
                              in_=x1b[:, h0 * TT:(h0 + 8) * TT]),
                  reads=[bx1[h0 + i] for i in range(8)],
                  writes=[X.dspill["xm_spill"][src_tile]],
                  tag=f"2spxm{src_tile}_{h0}", dma=True)
        if dg == 15 and src_tile == NTILE - 1:
            p.add("dve", lambda e: e.tensor_copy(
                out=xmlast[:],
                in_=x1b[:].rearrange("p (g t) -> p g t", g=G)[:, :, 511]),
                  reads=list(bx1), writes=[bxmlast], tag="2xml")

    def phaseB_group(tile, g):
        par = tile % 2
        s = g % 2
        i = g % 8
        if g == 8:
            load_p(tile, 1)
            load_vsr(tile, 1)
        p.add("pool", lambda e, i=i, s=s: e.tensor_tensor(
            out=pvs[s][:, :], in0=pyb[:, i * TT:(i + 1) * TT],
            in1=vh[:, i * TT:(i + 1) * TT], op=ALU.mult),
              reads=[bpy[i], bvh[i]], writes=[bpvs[s]], tag=f"2pv{g}")
        p.add("act", lambda e, g=g, i=i, s=s: e.activation(
            out=qscr[s][:, :], in_=pyb[:, i * TT:(i + 1) * TT],
            func=AF.Identity, scale=cvec["eu"][:, g:g + 1]),
              reads=[bpy[i], consts], writes=[bq[s]], tag=f"2q{g}")
        p.add("dve", lambda e, g=g, s=s: e.tensor_copy(
            out=a513[s][:, 1:2], in_=X.car2[:, g:g + 1]),
              reads=[X.bcar2[g]], writes=[ba513[s]], tag=f"2ac0{g}")
        p.add("dve", lambda e, g=g, s=s: e.tensor_tensor_scan(
            out=a513[s][:, 2:514],
            data0=cvec["lam"][:, g:g + 1].broadcast_to([128, TT]),
            data1=pvs[s][:, :], initial=X.car2[:, g:g + 1],
            op0=ALU.mult, op1=ALU.add),
              reads=[bpvs[s], X.bcar2[g], consts], writes=[ba513[s]],
              tag=f"2sa{g}")
        p.add("dve", lambda e, g=g, s=s: e.tensor_copy(
            out=b513[s][:, 1:2], in_=X.car2[:, 16 + g:17 + g]),
              reads=[X.bcar2[16 + g]], writes=[bb513[s]], tag=f"2bc0{g}")
        p.add("dve", lambda e, g=g, i=i, s=s: e.tensor_tensor_scan(
            out=b513[s][:, 2:514],
            data0=cvec["lam"][:, g:g + 1].broadcast_to([128, TT]),
            data1=pyb[:, i * TT:(i + 1) * TT],
            initial=X.car2[:, 16 + g:17 + g],
            op0=ALU.mult, op1=ALU.add),
              reads=[bpy[i], X.bcar2[16 + g], consts], writes=[bb513[s]],
              tag=f"2sb{g}")
        p.add("dve", lambda e, g=g, s=s: e.tensor_copy(
            out=X.car2[:, g:g + 1], in_=a513[s][:, 513:514]),
              reads=[ba513[s]], writes=[X.bcar2[g]], tag=f"2ca{g}")
        p.add("dve", lambda e, g=g, s=s: e.tensor_copy(
            out=X.car2[:, 16 + g:17 + g], in_=b513[s][:, 513:514]),
              reads=[bb513[s]], writes=[X.bcar2[16 + g]], tag=f"2cb{g}")
        p.add("pool", lambda e, s=s: e.tensor_tensor(
            out=dscr[s][:, :], in0=b513[s][:, 1:513], in1=qscr[s][:, :],
            op=ALU.add),
              reads=[bb513[s], bq[s]], writes=[bd[s]], tag=f"2dn{g}")
        p.add("act", lambda e, s=s: e.activation(
            out=dscr[s][:, :], in_=dscr[s][:, :], func=AF.Ln),
              reads=[bd[s]], writes=[bd[s]], tag=f"2ln{g}")
        p.add("act", lambda e, s=s: e.activation(
            out=dscr[s][:, :], in_=dscr[s][:, :], func=AF.Exp, scale=-1.0),
              reads=[bd[s]], writes=[bd[s]], tag=f"2ex{g}")
        p.add("pool", lambda e, i=i, s=s: e.tensor_tensor(
            out=qscr[s][:, :], in0=qscr[s][:, :],
            in1=vh[:, i * TT:(i + 1) * TT], op=ALU.mult),
              reads=[bq[s], bvh[i]], writes=[bq[s]], tag=f"2qv{g}")
        p.add("dve", lambda e, s=s: e.tensor_tensor(
            out=qscr[s][:, :], in0=qscr[s][:, :], in1=a513[s][:, 1:513],
            op=ALU.add),
              reads=[bq[s], ba513[s]], writes=[bq[s]], tag=f"2nm{g}")
        p.add("dve", lambda e, s=s: e.tensor_tensor(
            out=pvs[s][:, :], in0=qscr[s][:, :], in1=dscr[s][:, :],
            op=ALU.mult),
              reads=[bq[s], bd[s]], writes=[bpvs[s]], tag=f"2y{g}")
        p.add("dve", lambda e, g=g, i=i, par=par, s=s: e.tensor_tensor(
            out=zab[par][:, g * TT:(g + 1) * TT], in0=pvs[s][:, :],
            in1=srh[:, i * TT:(i + 1) * TT], op=ALU.mult),
              reads=[bpvs[s], bsrh[i]], writes=[bza[par][g]], tag=f"2za{g}")

    def wo_mm(tile, dg):
        par = tile % 2
        wpar = dg % 2
        blk = dr["wo_bf"][dg * 128:(dg + 1) * 128, :]
        p.add("sp", lambda e, blk=blk, wpar=wpar:
              e.dma_start(out=wbuf[wpar][:], in_=blk),
              reads=[X.dwbf["wo"]], writes=[bwbuf[wpar]],
              tag=f"2lwo{dg}", dma=True)
        bank = dg % 4

        def mmg(e, wpar=wpar, bank=bank, par=par):
            last = None
            for cg in range(G):
                last = e.matmul(out=PS[bank][:, :512],
                                lhsT=wbuf[wpar][:, cg * 128:(cg + 1) * 128],
                                rhs=zab[par][:, cg * TT:(cg + 1) * TT],
                                start=(cg == 0), stop=(cg == G - 1))
            return last
        p.add("pe", mmg, reads=[bwbuf[wpar]] + list(bza[par]),
              writes=[psb[bank]], tag=f"2mmo{tile}_{dg}")

    load_p(0, 0)
    load_x1(0)
    load_vsr(0, 0)
    for g in range(G):
        phaseB_group(0, g)
    for tile in range(NTILE):
        if tile + 1 < NTILE:
            load_p(tile + 1, 0)
            load_vsr(tile + 1, 0)
        for dg in range(G):
            wo_mm(tile, dg)
            if tile + 1 < NTILE:
                phaseB_group(tile + 1, dg)
            wo_evac(tile, dg)
            wo_add(tile, dg)
        if tile + 1 < NTILE:
            load_x1(tile + 1)

    p.add("pool", lambda e: e.dma_start(out=dr["cc2_in"], in_=xmlast[:]),
          reads=[bxmlast], writes=[X.dbuf["cc2_in"]], tag="cc2w", dma=True)
    p.add("pool", lambda e: e.collective_compute(
        "AllGather", ALU.bypass, replica_groups=[[0, 1, 2, 3, 4, 5, 6, 7]],
        ins=[dr["cc2_in"].opt()], outs=[dr["cc2_out"].opt()]),
          reads=[X.dbuf["cc2_in"]], writes=[X.dbuf["cc2_out"]], tag="ag2",
          dma=True, amount=1, collective=True)
    p.add("sp", lambda e: e.dma_start(
        out=X.gath2[:].rearrange("p (r c) -> p r c", r=8),
        in_=dr["cc2_out"].rearrange("(r p) c -> p r c", p=128)),
          reads=[X.dbuf["cc2_out"]], writes=[X.bgath2], tag="cc2r", dma=True)
    p.add("dve", lambda e: e.memset(xmprev[:], 0.0), writes=[bxmprev],
          tag="xmpz")
    for r in range(8):
        p.add("dve", lambda e, r=r: e.scalar_tensor_tensor(
            out=xmprev[:], in0=X.gath2[:, r * G:(r + 1) * G],
            scalar=X.mask8[:, r:r + 1], in1=xmprev[:],
            op0=ALU.mult, op1=ALU.add),
              reads=[X.bgath2, consts, bxmprev], writes=[bxmprev],
              tag=f"xmp{r}")
    X.col_ln(xmprev[:], bxmprev, X.xn2p_col[:], X.bxn2p,
             "ln2w", "ln2b", False, "bx2")
    if "dbg_bnd" in DEBUG_OUTS:
        p.add("act", lambda e: e.dma_start(out=dr["dbg_bnd"], in_=X.xn2p_col[:]),
              reads=[X.bxn2p], tag="dbg_bnd", dma=True)
    if "dbg_xmlast" in DEBUG_OUTS:
        p.add("act", lambda e: e.dma_start(out=dr["dbg_xmlast"], in_=xmlast[:]),
              reads=[bxmlast], tag="dbg_xmlast", dma=True)
    if "dbg_cc2in" in DEBUG_OUTS:
        p.add("act", lambda e: e.dma_start(out=dr["dbg_cc2in"], in_=dr["cc2_in"]),
              reads=[X.dbuf["cc2_in"]], tag="dbg_cc2in", dma=True)
    if "dbg_gath2" in DEBUG_OUTS:
        p.add("act", lambda e: e.dma_start(out=dr["dbg_gath2"], in_=X.gath2[:]),
              reads=[X.bgath2], tag="dbg_gath2", dma=True)
    if "dbg_xmprev" in DEBUG_OUTS:
        p.add("act", lambda e: e.dma_start(out=dr["dbg_xmprev"], in_=xmprev[:]),
              reads=[bxmprev], tag="dbg_xmprev", dma=True)
    s2.close()
    p.barrier()


def _emit_sweep2bi(X, top):
    nc, p, dr = X.nc, X.p, X.dr
    PS, psb = X.PS, X.psb
    cvec, ROW, rowsb = X.cvec, X.ROW, X.rowsb
    consts = X.consts

    sa = contextlib.ExitStack()

    def rba(name, shape, dt=F32):
        return sa.enter_context(nc.sbuf_tensor(name, shape, dt, side="right"))

    xmb = rba("xmb3", [128, G * TT])
    bxm = p.bufs_n("xm3_g", G)
    xn2 = rba("xn2b", [128, G * 513])
    bxn2 = p.bufs_n("xn2_g", G)
    sqs = [rba(f"sqs3{i}", [128, 512]) for i in range(2)]
    bsqs = p.bufs_n("sqs3", 2)
    ckb = rba("ckb", [128, G * TT], BF16)
    bck = p.bufs_n("ck_g", G)
    crb = [rba(f"crb{i}", [128, G * TT], BF16) for i in range(2)]
    bcr = [p.bufs_n(f"cr{i}_g", G) for i in range(2)]
    srkb = rba("srkb", [128, G * TT])
    bsrk = p.bufs_n("srk_g", G)
    wbuf = [rba(f"wbuf3{i}", [128, C], BF16) for i in range(2)]
    bwbuf = p.bufs_n("wbuf3", 2)
    dscr = rba("dscr3", [128, 512])
    bd = p.buf("dscr3")

    xm_view = xmb[:].rearrange("p (g t) -> p g t", g=G)
    xn2_view = xn2[:].rearrange("p (g c) -> p g c", g=G)

    def x2_g(g, c0, c1):
        return xn2[:, g * 513 + c0: g * 513 + c1]

    def bi_prep_head(tile):
        p.add("sp", lambda e, tile=tile: e.dma_start(
            out=xmb[:], in_=X.spsl("xm_spill", tile, 0, G)),
              reads=[X.dspill["xm_spill"][tile]], writes=list(bxm),
              tag=f"3ldxm{tile}", dma=True)
        if tile == 0:
            p.add("dve", lambda e: e.tensor_copy(out=xn2_view[:, :, 0],
                                                 in_=X.xn2p_col[:]),
                  reads=[X.bxn2p], writes=list(bxn2), tag=f"3bnd{tile}")
        else:
            p.add("dve", lambda e: e.tensor_copy(out=xn2_view[:, :, 0],
                                                 in_=X.xn2carry[:]),
                  reads=[X.bxn2carry], writes=list(bxn2), tag=f"3bnd{tile}")
        X.stats_into_ps45([xm_view[:, g] for g in range(G)], bxm, sqs, bsqs)
        X.row_stats(double=False)
        X.bcast_row(ROW["A"], 4, f"3A{tile}")
        X.bcast_row(ROW["Br"], 5, f"3B{tile}")

    def bi_prep_g(tile, g):
        par = tile % 2
        p.add("dve", lambda e, g=g: e.tensor_tensor(
            out=x2_g(g, 1, 513), in0=xm_view[:, g], in1=PS[4][:, :512],
            op=ALU.mult),
              reads=[bxm[g], psb[4]], writes=[bxn2[g]], tag=f"3n1_{g}")
        p.add("dve", lambda e, g=g: e.tensor_tensor(
            out=x2_g(g, 1, 513), in0=x2_g(g, 1, 513), in1=PS[5][:, :512],
            op=ALU.add),
              reads=[bxn2[g], psb[5]], writes=[bxn2[g]], tag=f"3n2_{g}")
        p.add("dve", lambda e, g=g: e.tensor_scalar(
            out=x2_g(g, 1, 513), in0=x2_g(g, 1, 513),
            scalar1=cvec["ln2w"][:, g:g + 1], scalar2=cvec["ln2b"][:, g:g + 1],
            op0=ALU.mult, op1=ALU.add),
              reads=[bxn2[g], consts], writes=[bxn2[g]], tag=f"3n3_{g}")
        p.add("dve", lambda e, g=g: e.tensor_tensor(
            out=dscr[:, :], in0=x2_g(g, 1, 513), in1=x2_g(g, 0, 512),
            op=ALU.subtract),
              reads=[bxn2[g]], writes=[bd], tag=f"3df{g}")
        p.add("dve", lambda e, g=g: e.scalar_tensor_tensor(
            out=ckb[:, g * TT:(g + 1) * TT], in0=dscr[:, :],
            scalar=cvec["cmk"][:, g:g + 1], in1=x2_g(g, 0, 512),
            op0=ALU.mult, op1=ALU.add),
              reads=[bd, bxn2[g], consts], writes=[bck[g]], tag=f"3mk{g}")
        p.add("dve", lambda e, g=g, par=par: e.scalar_tensor_tensor(
            out=crb[par][:, g * TT:(g + 1) * TT], in0=dscr[:, :],
            scalar=cvec["cmr"][:, g:g + 1], in1=x2_g(g, 0, 512),
            op0=ALU.mult, op1=ALU.add),
              reads=[bd, bxn2[g], consts], writes=[bcr[par][g]], tag=f"3mr{g}")

    def bi_prep_tail(tile):
        p.add("dve", lambda e: e.tensor_copy(out=X.xn2carry[:],
                                             in_=xn2_view[:, :, 512]),
              reads=list(bxn2), writes=[X.bxn2carry], tag=f"3xnc{tile}")
        p.add("act", lambda e, tile=tile: e.dma_start(
            out=X.spsl("ck_spill", tile, 0, G), in_=ckb[:]),
              reads=list(bck), writes=[X.dspill["ck_spill"][tile]],
              tag=f"3spck{tile}", dma=True)

    def bi_stream_g(tile, dg):
        par = tile % 2
        wpar = dg % 2
        blk = dr["wcr_bf"][dg * 128:(dg + 1) * 128, :]
        p.add("sp", lambda e, blk=blk, wpar=wpar:
              e.dma_start(out=wbuf[wpar][:], in_=blk),
              reads=[X.dwbf["wcr"]], writes=[bwbuf[wpar]],
              tag=f"3lwcr{dg}", dma=True)
        bank = dg % 2

        def mmg(e, wpar=wpar, bank=bank, par=par):
            last = None
            for cg in range(G):
                last = e.matmul(out=PS[bank][:, :512],
                                lhsT=wbuf[wpar][:, cg * 128:(cg + 1) * 128],
                                rhs=crb[par][:, cg * TT:(cg + 1) * TT],
                                start=(cg == 0), stop=(cg == G - 1))
            return last
        p.add("pe", mmg, reads=[bwbuf[wpar]] + list(bcr[par]),
              writes=[psb[bank]], tag=f"3mmcr{dg}")
        p.add("act", lambda e, dg=dg, bank=bank: e.activation(
            out=srkb[:, dg * TT:(dg + 1) * TT], in_=PS[bank][:, :512],
            func=AF.Sigmoid),
              reads=[psb[bank]], writes=[bsrk[dg]], tag=f"3sr{dg}")
        if dg == 7 or dg == 15:
            h0 = dg - 7
            p.add("act", lambda e, tile=tile, h0=h0: e.dma_start(
                out=X.spsl("srk_spill", tile, h0, h0 + 8),
                in_=srkb[:, h0 * TT:(h0 + 8) * TT]),
                  reads=[bsrk[h0 + i] for i in range(8)],
                  writes=[X.dspill["srk_spill"][tile]],
                  tag=f"3spsr{tile}_{h0}", dma=True)

    bi_prep_head(0)
    for g in range(G):
        bi_prep_g(0, g)
    bi_prep_tail(0)
    for tile in range(NTILE):
        for dg in range(G):
            bi_stream_g(tile, dg)
            if tile + 1 < NTILE:
                if dg == 0:
                    bi_prep_head(tile + 1)
                elif 2 <= dg < 10:
                    bi_prep_g(tile + 1, 2 * (dg - 2))
                    bi_prep_g(tile + 1, 2 * (dg - 2) + 1)
                elif dg == 10:
                    bi_prep_tail(tile + 1)
    sa.close()
    p.barrier()

    sb_ = contextlib.ExitStack()

    def rbb(name, shape, dt=F32):
        return sb_.enter_context(nc.sbuf_tensor(name, shape, dt, side="right"))

    xmb2 = rbb("xmb4", [128, G * TT])
    bxm2 = p.bufs_n("xm4_g", G)
    ckb2 = [rbb(f"ckb4{i}", [128, G * TT], BF16) for i in range(2)]
    bck2 = [p.bufs_n(f"ck4{i}_g", G) for i in range(2)]
    srkb2 = rbb("srkb4", [128, G * TT])
    bsrk2 = p.bufs_n("srk4_g", G)
    hsb = rbb("hsb", [128, HG * TT], BF16)
    bh = p.bufs_n("h_g", HG)
    wck_buf = [rbb(f"wckb{i}", [128, C], BF16) for i in range(2)]
    bwck = p.bufs_n("wckb", 2)
    wcv_buf = [rbb(f"wcvb{i}", [128, 16 * 128], BF16) for i in range(2)]
    bwcv = p.bufs_n("wcvb", 2)
    hscr = [rbb(f"hscr{i}", [128, 512]) for i in range(2)]
    bhscr = p.bufs_n("hscr", 2)
    rkvs = rbb("rkvs", [128, 512])
    brkv = p.buf("rkvs")

    xm2_view = xmb2[:].rearrange("p (g t) -> p g t", g=G)

    def bi_b_loads(tile):
        par = tile % 2
        p.add("sp", lambda e, tile=tile, par=par: e.dma_start(
            out=ckb2[par][:], in_=X.spsl("ck_spill", tile, 0, G)),
              reads=[X.dspill["ck_spill"][tile]], writes=list(bck2[par]),
              tag=f"4ldck{tile}", dma=True)

    def bi_b_tile(tile):
        par = tile % 2
        p.add("sp", lambda e, tile=tile: e.dma_start(
            out=xmb2[:], in_=X.spsl("xm_spill", tile, 0, G)),
              reads=[X.dspill["xm_spill"][tile]], writes=list(bxm2),
              tag=f"4ldxm{tile}", dma=True)
        p.add("sp", lambda e, tile=tile: e.dma_start(
            out=srkb2[:], in_=X.spsl("srk_spill", tile, 0, G)),
              reads=[X.dspill["srk_spill"][tile]], writes=list(bsrk2),
              tag=f"4ldsr{tile}", dma=True)
        for hg in range(HG):
            wpar = hg % 2
            blk = dr["wck_bf"][hg * 128:(hg + 1) * 128, :]
            p.add("sp", lambda e, blk=blk, wpar=wpar:
                  e.dma_start(out=wck_buf[wpar][:], in_=blk),
                  reads=[X.dwbf["wck"]], writes=[bwck[wpar]],
                  tag=f"4lwck{hg}", dma=True)
            bank = hg % 2

            def mmg(e, wpar=wpar, bank=bank, par=par):
                last = None
                for cg in range(G):
                    last = e.matmul(out=PS[bank][:, :512],
                                    lhsT=wck_buf[wpar][:, cg * 128:(cg + 1) * 128],
                                    rhs=ckb2[par][:, cg * TT:(cg + 1) * TT],
                                    start=(cg == 0), stop=(cg == G - 1))
                return last
            p.add("pe", mmg, reads=[bwck[wpar]] + list(bck2[par]),
                  writes=[psb[bank]], tag=f"4mmck{hg}")
            spar = hg % 2
            p.add("act", lambda e, bank=bank, spar=spar: e.activation(
                out=hscr[spar][:, :], in_=PS[bank][:, :512], func=AF.Relu),
                  reads=[psb[bank]], writes=[bhscr[spar]], tag=f"4rl{hg}")
            p.add("dve", lambda e, hg=hg, spar=spar: e.tensor_tensor(
                out=hsb[:, hg * TT:(hg + 1) * TT], in0=hscr[spar][:, :],
                in1=hscr[spar][:, :], op=ALU.mult),
                  reads=[bhscr[spar]], writes=[bh[hg]], tag=f"4sq{hg}")
            if hg == 32 and tile + 1 < NTILE:
                bi_b_loads(tile + 1)

        for cg in range(G):
            bank = 2 + cg % 2
            for qtr in range(4):
                wpar = qtr % 2
                blk = dr["wcv_bf"][cg * 128:(cg + 1) * 128,
                                   qtr * 16 * 128:(qtr + 1) * 16 * 128]
                p.add("sp", lambda e, blk=blk, wpar=wpar:
                      e.dma_start(out=wcv_buf[wpar][:], in_=blk),
                      reads=[X.dwbf["wcv"]], writes=[bwcv[wpar]],
                      tag=f"4lwcv{cg}_{qtr}", dma=True)

                def mmg2(e, cg=cg, qtr=qtr, wpar=wpar):
                    last = None
                    for hh in range(16):
                        hg = qtr * 16 + hh
                        last = e.matmul(out=PS[2 + cg % 2][:, :512],
                                        lhsT=wcv_buf[wpar][:, hh * 128:(hh + 1) * 128],
                                        rhs=hsb[:, hg * TT:(hg + 1) * TT],
                                        start=(hg == 0), stop=(hg == HG - 1))
                    return last
                p.add("pe", mmg2,
                      reads=[bwcv[wpar]] + [bh[qtr * 16 + i] for i in range(16)],
                      writes=[psb[bank]], tag=f"4mmcv{cg}_{qtr}")
            p.add("dve", lambda e, cg=cg, bank=bank: e.tensor_tensor(
                out=rkvs[:, :], in0=PS[bank][:, :512],
                in1=srkb2[:, cg * TT:(cg + 1) * TT], op=ALU.mult),
                  reads=[psb[bank], bsrk2[cg]], writes=[brkv], tag=f"4rkv{cg}")
            p.add("dve", lambda e, cg=cg: e.tensor_tensor(
                out=xm2_view[:, cg], in0=xm2_view[:, cg], in1=rkvs[:, :],
                op=ALU.add),
                  reads=[bxm2[cg], brkv], writes=[bxm2[cg]], tag=f"4at{cg}")

        p.add("act", lambda e, tile=tile: e.dma_start(
            out=X.spsl("at_spill", tile, 0, G), in_=xmb2[:]),
              reads=list(bxm2), writes=[X.dspill["at_spill"][tile]],
              tag=f"4spat{tile}", dma=True)

    bi_b_loads(0)
    for tile in range(NTILE):
        bi_b_tile(tile)

    if "dbg_at" in DEBUG_OUTS:
        for tile in range(NTILE):
            p.add("act", lambda e, tile=tile: e.dma_start(
                out=dr["dbg_at"][:, tile * G * TT:(tile + 1) * G * TT]
                .rearrange("p (g t) -> p g t", g=G),
                in_=dr["at_spill"].rearrange("p (g t) -> p g t", g=G)
                [:, :, tile * TT:(tile + 1) * TT]),
                  reads=[X.dspill["at_spill"][tile]], tag=f"dbgat{tile}",
                  dma=True)

    sb_.close()
    p.barrier()


def _emit_sweep2bii(X, top):
    nc, p, dr = X.nc, X.p, X.dr
    PS, psb = X.PS, X.psb
    cvec, gvec, ROW, rowsb = X.cvec, X.gvec, X.ROW, X.rowsb
    consts = X.consts

    sa = contextlib.ExitStack()

    def rba(name, shape, dt=F32):
        return sa.enter_context(nc.sbuf_tensor(name, shape, dt, side="right"))

    atb = rba("atb5", [128, G * TT])
    bat = p.bufs_n("at5_g", G)
    xfm = rba("xfm5", [128, G * TT])
    bxfm = p.bufs_n("xfm5_g", G)
    xfb = [rba(f"xfb5{i}", [128, G * TT], BF16) for i in range(2)]
    bxf = [p.bufs_n(f"xf5{i}_g", G) for i in range(2)]
    fhb = rba("fhb5", [128, GI * TT], BF16)
    bfh = p.bufs_n("fh5_g", GI)
    sqs = [rba(f"sqs5{i}", [128, 512]) for i in range(2)]
    bsqs = p.bufs_n("sqs5", 2)
    wg_buf = [rba(f"wgb{i}", [128, C], BF16) for i in range(2)]
    bwg = p.bufs_n("wgb", 2)
    wu_buf = [rba(f"wub{i}", [128, C], BF16) for i in range(2)]
    bwu = p.bufs_n("wub", 2)
    gscr = [rba(f"gscr{i}", [128, 512]) for i in range(2)]
    bgscr = p.bufs_n("gscr", 2)
    uscr = [rba(f"uscr{i}", [128, 512]) for i in range(2)]
    buscr = p.bufs_n("uscr", 2)
    tscr = rba("tscr5", [128, 512])
    bts = p.buf("tscr5")

    at_view = atb[:].rearrange("p (g t) -> p g t", g=G)
    xfm_view = xfm[:].rearrange("p (g t) -> p g t", g=G)

    def iia_prep_head(tile):
        p.add("sp", lambda e, tile=tile: e.dma_start(
            out=atb[:], in_=X.spsl("at_spill", tile, 0, G)),
              reads=[X.dspill["at_spill"][tile]], writes=list(bat),
              tag=f"5ldat{tile}", dma=True)
        p.add("sp", lambda e, tile=tile: e.dma_start(
            out=xfm[:], in_=X.xsl(tile)),
              writes=list(bxfm), tag=f"5ldx{tile}", dma=True)
        X.stats_into_ps45([at_view[:, g] for g in range(G)], bat, sqs, bsqs)
        X.row_stats(double=False)
        X.bcast_row(ROW["A"], 4, f"5A{tile}")
        X.bcast_row(ROW["Br"], 5, f"5B{tile}")

    def iia_prep_lnat(tile, g):
        p.add("dve", lambda e, g=g: e.tensor_tensor(
            out=at_view[:, g], in0=at_view[:, g], in1=PS[4][:, :512],
            op=ALU.mult),
              reads=[bat[g], psb[4]], writes=[bat[g]], tag=f"5l1{g}")
        p.add("dve", lambda e, g=g: e.tensor_tensor(
            out=at_view[:, g], in0=at_view[:, g], in1=PS[5][:, :512],
            op=ALU.add),
              reads=[bat[g], psb[5]], writes=[bat[g]], tag=f"5l2{g}")
        p.add("dve", lambda e, g=g: e.tensor_tensor(
            out=at_view[:, g], in0=at_view[:, g], in1=xfm_view[:, g],
            op=ALU.add),
              reads=[bat[g], bxfm[g]], writes=[bat[g]], tag=f"5xa{g}")

    def iia_prep_mid(tile):
        X.stats_into_ps45([at_view[:, g] for g in range(G)], bat, sqs, bsqs)
        X.row_stats(double=False)
        X.bcast_row(ROW["A"], 4, f"5A2{tile}")
        X.bcast_row(ROW["Br"], 5, f"5B2{tile}")

    def iia_prep_xf(tile, g):
        par = tile % 2
        p.add("dve", lambda e, g=g: e.tensor_tensor(
            out=tscr[:, :], in0=at_view[:, g], in1=PS[4][:, :512],
            op=ALU.mult),
              reads=[bat[g], psb[4]], writes=[bts], tag=f"5f1{g}")
        p.add("dve", lambda e, g=g, par=par: e.tensor_tensor(
            out=xfb[par][:, g * TT:(g + 1) * TT], in0=tscr[:, :],
            in1=PS[5][:, :512], op=ALU.add),
              reads=[bts, psb[5]], writes=[bxf[par][g]], tag=f"5f2{g}")

    def iia_prep_tail(tile):
        p.add("act", lambda e, tile=tile: e.dma_start(
            out=X.spsl("xa_spill", tile, 0, G), in_=atb[:]),
              reads=list(bat), writes=[X.dspill["xa_spill"][tile]],
              tag=f"5spxa{tile}", dma=True)

    def iia_stream_ig(tile, ig):
        par = tile % 2
        wpar = ig % 2
        gblk = dr["wg_bf"][ig * 128:(ig + 1) * 128, :]
        ublk = dr["wu_bf"][ig * 128:(ig + 1) * 128, :]
        p.add("sp", lambda e, gblk=gblk, wpar=wpar:
              e.dma_start(out=wg_buf[wpar][:], in_=gblk),
              reads=[X.dwbf["wg"]], writes=[bwg[wpar]],
              tag=f"5lwg{ig}", dma=True)
        p.add("sp", lambda e, ublk=ublk, wpar=wpar:
              e.dma_start(out=wu_buf[wpar][:], in_=ublk),
              reads=[X.dwbf["wu"]], writes=[bwu[wpar]],
              tag=f"5lwu{ig}", dma=True)

        def mmg_g(e, wpar=wpar, bank=ig % 2, par=par):
            last = None
            for cg in range(G):
                last = e.matmul(out=PS[bank][:, :512],
                                lhsT=wg_buf[wpar][:, cg * 128:(cg + 1) * 128],
                                rhs=xfb[par][:, cg * TT:(cg + 1) * TT],
                                start=(cg == 0), stop=(cg == G - 1))
            return last
        p.add("pe", mmg_g, reads=[bwg[wpar]] + list(bxf[par]),
              writes=[psb[ig % 2]], tag=f"5mmg{ig}")

        def mmg_u(e, wpar=wpar, bank=2 + ig % 2, par=par):
            last = None
            for cg in range(G):
                last = e.matmul(out=PS[bank][:, :512],
                                lhsT=wu_buf[wpar][:, cg * 128:(cg + 1) * 128],
                                rhs=xfb[par][:, cg * TT:(cg + 1) * TT],
                                start=(cg == 0), stop=(cg == G - 1))
            return last
        p.add("pe", mmg_u, reads=[bwu[wpar]] + list(bxf[par]),
              writes=[psb[2 + ig % 2]], tag=f"5mmu{ig}")
        spar = ig % 2
        p.add("act", lambda e, ig=ig, spar=spar: e.activation(
            out=gscr[spar][:, :], in_=PS[ig % 2][:, :512], func=AF.Silu,
            scale=gvec["gates"][:, ig:ig + 1]),
              reads=[psb[ig % 2], consts], writes=[bgscr[spar]],
              tag=f"5sg{ig}")
        p.add("act", lambda e, ig=ig, spar=spar: e.activation(
            out=uscr[spar][:, :], in_=PS[2 + ig % 2][:, :512],
            func=AF.Identity, scale=gvec["ups"][:, ig:ig + 1]),
              reads=[psb[2 + ig % 2], consts], writes=[buscr[spar]],
              tag=f"5su{ig}")
        p.add("dve", lambda e, ig=ig, spar=spar: e.tensor_tensor(
            out=fhb[:, ig * TT:(ig + 1) * TT], in0=gscr[spar][:, :],
            in1=uscr[spar][:, :], op=ALU.mult),
              reads=[bgscr[spar], buscr[spar]], writes=[bfh[ig]],
              tag=f"5hd{ig}")
        if ig == 21 or ig == GI - 1:
            i0 = 0 if ig == 21 else 22
            i1 = 22 if ig == 21 else GI
            p.add("act", lambda e, tile=tile, i0=i0, i1=i1: e.dma_start(
                out=X.spsl("fh_spill", tile, i0, i1, ng=GI),
                in_=fhb[:, i0 * TT:i1 * TT]),
                  reads=[bfh[i] for i in range(i0, i1)],
                  writes=[X.dspill["fh_spill"][tile]],
                  tag=f"5spfh{tile}_{i0}", dma=True)

    iia_prep_head(0)
    for g in range(G):
        iia_prep_lnat(0, g)
    iia_prep_mid(0)
    for g in range(G):
        iia_prep_xf(0, g)
    iia_prep_tail(0)
    for tile in range(NTILE):
        for ig in range(GI):
            iia_stream_ig(tile, ig)
            if tile + 1 < NTILE:
                if ig == 0:
                    iia_prep_head(tile + 1)
                elif 2 <= ig < 10:
                    iia_prep_lnat(tile + 1, 2 * (ig - 2))
                    iia_prep_lnat(tile + 1, 2 * (ig - 2) + 1)
                elif ig == 10:
                    iia_prep_mid(tile + 1)
                elif 12 <= ig < 20:
                    iia_prep_xf(tile + 1, 2 * (ig - 12))
                    iia_prep_xf(tile + 1, 2 * (ig - 12) + 1)
                elif ig == 20:
                    iia_prep_tail(tile + 1)
    sa.close()
    p.barrier()

    sb_ = contextlib.ExitStack()

    def rbb(name, shape, dt=F32):
        return sb_.enter_context(nc.sbuf_tensor(name, shape, dt, side="right"))

    xab = rbb("xab6", [128, G * TT])
    bxa = p.bufs_n("xa6_g", G)
    fhb2 = rbb("fhb6", [128, GI * TT], BF16)
    bfh2 = p.bufs_n("fh6_g", GI)
    ffn = rbb("ffn6", [128, G * TT])
    bffn = p.bufs_n("ffn6_g", G)
    wd_buf = [rbb(f"wdb{i}", [128, NI], BF16) for i in range(2)]
    bwd = p.bufs_n("wdb", 2)
    sqs = [rbb(f"sqs6{i}", [128, 512]) for i in range(2)]
    bsqs = p.bufs_n("sqs6", 2)

    xa_view = xab[:].rearrange("p (g t) -> p g t", g=G)
    ffn_view = ffn[:].rearrange("p (g t) -> p g t", g=G)

    def iib_tile(tile):
        p.add("sp", lambda e, tile=tile: e.dma_start(
            out=xab[:], in_=X.spsl("xa_spill", tile, 0, G)),
              reads=[X.dspill["xa_spill"][tile]], writes=list(bxa),
              tag=f"6ldxa{tile}", dma=True)
        p.add("sp", lambda e, tile=tile: e.dma_start(
            out=fhb2[:], in_=X.spsl("fh_spill", tile, 0, GI, ng=GI)),
              reads=[X.dspill["fh_spill"][tile]], writes=list(bfh2),
              tag=f"6ldfh{tile}", dma=True)

        for cg in range(G):
            wpar = cg % 2
            blk = dr["wd_bf"][cg * 128:(cg + 1) * 128, :]
            p.add("sp", lambda e, blk=blk, wpar=wpar:
                  e.dma_start(out=wd_buf[wpar][:], in_=blk),
                  reads=[X.dwbf["wd"]], writes=[bwd[wpar]],
                  tag=f"6lwd{cg}", dma=True)
            bank = cg % 2

            def mmg(e, wpar=wpar, bank=bank):
                last = None
                for ig in range(GI):
                    last = e.matmul(out=PS[bank][:, :512],
                                    lhsT=wd_buf[wpar][:, ig * 128:(ig + 1) * 128],
                                    rhs=fhb2[:, ig * TT:(ig + 1) * TT],
                                    start=(ig == 0), stop=(ig == GI - 1))
                return last
            p.add("pe", mmg, reads=[bwd[wpar]] + list(bfh2),
                  writes=[psb[bank]], tag=f"6mmd{cg}")
            p.add("act", lambda e, cg=cg, bank=bank: e.activation(
                out=ffn_view[:, cg], in_=PS[bank][:, :512], func=AF.Identity,
                scale=cvec["downs"][:, cg:cg + 1]),
                  reads=[psb[bank], consts], writes=[bffn[cg]], tag=f"6ev{cg}")

        X.stats_into_ps45([ffn_view[:, g] for g in range(G)], bffn, sqs, bsqs)
        X.row_stats(double=False)
        X.bcast_row(ROW["A"], 4, "A")
        X.bcast_row(ROW["Br"], 5, "B")
        for g in range(G):
            p.add("dve", lambda e, g=g: e.tensor_tensor(
                out=ffn_view[:, g], in0=ffn_view[:, g], in1=PS[4][:, :512],
                op=ALU.mult),
                  reads=[bffn[g], psb[4]], writes=[bffn[g]], tag=f"6l1{g}")
            p.add("dve", lambda e, g=g: e.tensor_tensor(
                out=ffn_view[:, g], in0=ffn_view[:, g], in1=PS[5][:, :512],
                op=ALU.add),
                  reads=[bffn[g], psb[5]], writes=[bffn[g]], tag=f"6l2{g}")
            p.add("dve", lambda e, g=g: e.tensor_tensor(
                out=ffn_view[:, g], in0=ffn_view[:, g], in1=xa_view[:, g],
                op=ALU.add),
                  reads=[bffn[g], bxa[g]], writes=[bffn[g]], tag=f"6out{g}")

        p.add("act", lambda e, tile=tile: e.dma_start(
            out=X.ysl(tile), in_=ffn[:]),
              reads=list(bffn), writes=[], tag=f"6sty{tile}", dma=True)

    for tile in range(NTILE):
        iib_tile(tile)
    sb_.close()


_NC_CACHE = {}


def _get_nc():
    key = (BUILD_STAGE, tuple(DEBUG_OUTS))
    if key not in _NC_CACHE:
        _NC_CACHE[key] = build_nc()
    return _NC_CACHE[key]


def prep_weights(inp):
    w = {}
    w["ident"] = np.eye(128, dtype=np.float32)
    w["ones"] = np.ones((128, 1), np.float32)
    for nm, key in (("ln1w", "L0_attn_ln1_w"), ("ln1b", "L0_attn_ln1_b"),
                    ("ln2w", "L0_attn_ln2_w"), ("ln2b", "L0_attn_ln2_b"),
                    ("tmk", "L0_attn_tmk"), ("tmv", "L0_attn_tmv"),
                    ("tmr", "L0_attn_tmr"), ("cmk", "L0_attn_cmk"),
                    ("cmr", "L0_attn_cmr"), ("td", "L0_attn_time_decay"),
                    ("tf", "L0_attn_time_first"), ("downs", "L0_ffn_down_s")):
        w[nm] = _fm(np.asarray(inp[key], np.float32))
    w["gates"] = _fm(np.asarray(inp["L0_ffn_gate_s"], np.float32))
    w["ups"] = _fm(np.asarray(inp["L0_ffn_up_s"], np.float32))
    for nm, key in (("wk", "L0_attn_Wk"), ("wv", "L0_attn_Wv"),
                    ("wr", "L0_attn_Wr"), ("wo", "L0_attn_Wo"),
                    ("wcr", "L0_attn_Wcr"), ("wck", "L0_attn_Wck"),
                    ("wcv", "L0_attn_Wcv"), ("wg", "L0_ffn_gate_w"),
                    ("wu", "L0_ffn_up_w"), ("wd", "L0_ffn_down_w")):
        w[nm] = _warp(np.asarray(inp[key], np.float32))
    return w


def make_in_maps(L0_attn_x, weights):
    x = np.asarray(L0_attn_x, np.float32)
    in_maps = []
    for core in range(8):
        s, slab = core // 4, core % 4
        t0 = slab * TCORE
        m = dict(weights)
        xs = x[s, t0:t0 + TCORE]
        m["xt"] = np.ascontiguousarray(
            xs.reshape(NTILE, TT, G, 128).transpose(3, 0, 2, 1)
            .reshape(128, NTILE * G * TT))
        m["xprev"] = (np.zeros((128, G), np.float32) if slab == 0
                      else np.ascontiguousarray(
                          x[s, t0 - 1].reshape(G, 128).T))
        m["bflag"] = np.full((128, 1), 0.0 if slab == 0 else 1.0, np.float32)
        dc = np.full(8, -1e30, np.float32)
        mk = np.zeros(8, np.float32)
        for r in range(8):
            rs, rslab = r // 4, r % 4
            if rs == s and rslab < slab:
                dc[r] = -4096.0 * (slab - 1 - rslab)
        if slab > 0:
            mk[core - 1] = 1.0
        m["dc8"] = np.ascontiguousarray(np.tile(dc[None, :], (128, 1)))
        m["mask8"] = np.ascontiguousarray(np.tile(mk[None, :], (128, 1)))
        in_maps.append(m)
    return in_maps


def kernel(**inputs):
    assert BUILD_STAGE == 4, "kernel() requires the full program"
    nc = _get_nc()
    w = prep_weights(inputs)
    in_maps = make_in_maps(inputs["L0_attn_x"], w)
    res = run_bass_kernel_spmd(nc, in_maps, core_ids=list(range(8)))
    out = np.empty((B, T, C), np.float32)
    for core in range(8):
        s, slab = core // 4, core % 4
        yc = np.asarray(res.results[core]["y"], np.float32)
        out[s, slab * TCORE:(slab + 1) * TCORE] = (
            yc.reshape(128, NTILE, G, TT).transpose(1, 3, 2, 0)
            .reshape(TCORE, C))
    return out


def run_stage(inputs, stage, debug):
    global BUILD_STAGE, DEBUG_OUTS
    BUILD_STAGE, DEBUG_OUTS = stage, list(debug)
    nc = _get_nc()
    w = prep_weights(inputs)
    in_maps = make_in_maps(inputs["L0_attn_x"], w)
    return run_bass_kernel_spmd(nc, in_maps, core_ids=list(range(8)))
